# version 03
# baseline (speedup 1.0000x reference)
import numpy as np
import ml_dtypes
from contextlib import ExitStack
import concourse.bass as bass
import concourse.mybir as mybir

F32 = mybir.dt.float32
BF16 = mybir.dt.bfloat16
U8 = mybir.dt.uint8
AF = mybir.ActivationFunctionType
ALU = mybir.AluOpType
AX = mybir.AxisListType

ENGS = ("sync", "scalar", "vector", "tensor", "gpsimd")


class Prog:

    def __init__(self, name="prog"):
        self.nc = bass.Bass("TRN2", target_bir_lowering=False)
        self.st = ExitStack()
        self.ops = {e: [] for e in ENGS}
        self.semv = {}
        self.sems = {}
        self.name = name

    def sem(self, name):
        self.sems[name] = self.st.enter_context(self.nc.semaphore(name))
        self.semv[name] = 0
        return name

    def sbuf(self, name, shape, dt):
        return self.st.enter_context(self.nc.sbuf_tensor(name, shape, dt))

    def psum(self, name, shape, dt=F32):
        return self.st.enter_context(self.nc.psum_tensor(name, shape, dt))

    def wait(self, eng, sem, val=None):
        v = self.semv[sem] if val is None else val
        if v > 0:
            self.ops[eng].append(("w", sem, v))

    def op(self, eng, fn, inc=None, k=1):
        if inc is not None:
            self.semv[inc] += k
        self.ops[eng].append(("o", fn, inc, k,
                              self.semv[inc] if inc is not None else 0))

    def dma(self, eng, out, in_, sem, slow=False):
        self.semv[sem] += 16
        o, i = out, in_
        self.ops[eng].append(("o", lambda e: e.dma_start(
            out=o, in_=i, allow_slow_non_contiguous=slow), sem, 16, None))

    def build(self):
        with self.nc.Block() as block:
            for eng in ENGS:
                if not self.ops[eng]:
                    continue
                items = self.ops[eng]

                auto_self = eng in ("vector", "scalar", "gpsimd")
                own = None
                own_count = [0]
                if auto_self:
                    own = self.st.enter_context(self.nc.semaphore(f"own_{eng}"))

                def body(h, items=items, auto_self=auto_self, own=own,
                         own_count=own_count):
                    hwm = {}
                    pending = None
                    for it in items:
                        if it[0] == "w":
                            _, s, v = it
                            if hwm.get(s, -1) < v:
                                h.wait_ge(self.sems[s], v)
                                hwm[s] = v
                        else:
                            _, fn, inc, k, postv = it
                            if auto_self and pending is not None:
                                ps, pv = pending
                                key = ps if isinstance(ps, str) else "__own__"
                                if hwm.get(key, -1) < pv:
                                    hh = self.sems[ps] if isinstance(ps, str) else ps
                                    h.wait_ge(hh, pv)
                                    hwm[key] = pv
                                pending = None
                            instr = fn(h)
                            if inc is not None:
                                instr.then_inc(self.sems[inc], k)
                                if auto_self:
                                    pending = (inc, postv) if postv is not None else None
                            elif auto_self:
                                own_count[0] += 1
                                instr.then_inc(own, 1)
                                pending = (own, own_count[0])

                getattr(block, eng)(body)
        self.st.close()
        return self.nc


def np_bf16(x):
    return np.ascontiguousarray(x).astype(ml_dtypes.bfloat16)


class NS:
    pass



def emit_ln(p, s, pa, pb, T, nch, eps=1e-5, rms=False):
    D = nch * 128
    nth = T // 512
    for c in range(nch):
        sl = slice(c * T, (c + 1) * T)
        p.op("vector", lambda h, sl=sl: h.tensor_tensor(
            out=s.xn[:, sl], in0=s.xb[:, sl], in1=s.xb[:, sl], op=ALU.mult),
            inc="dv" if c == nch - 1 else None)
    for th in range(nth):
        p.wait("tensor", "dv")
        for c in range(nch):
            k0 = c * T + th * 512
            if not rms:
                p.op("tensor", lambda h, k0=k0, st=(c == 0), sp=(c == nch - 1): h.matmul(
                    pa[0:1, :], s.ones_l[:, 0:1], s.xb[:, k0:k0 + 512], start=st, stop=sp))
            p.op("tensor", lambda h, k0=k0, st=(c == 0), sp=(c == nch - 1): h.matmul(
                pb[0:1, :], s.ones_l[:, 0:1], s.xn[:, k0:k0 + 512], start=st, stop=sp),
                inc="pe" if c == nch - 1 else None)
        tsl = slice(th * 512, th * 512 + 512)
        p.wait("vector", "pe")
        if not rms:
            p.op("vector", lambda h, tsl=tsl: h.tensor_scalar(
                out=s.r0[0:1, tsl], in0=pa[0:1, :], scalar1=1.0 / D, scalar2=None,
                op0=ALU.mult))
        p.op("vector", lambda h, tsl=tsl: h.tensor_scalar(
            out=s.r1[0:1, tsl], in0=pb[0:1, :], scalar1=1.0 / D, scalar2=None,
            op0=ALU.mult), inc="dv")
    if not rms:
        p.op("vector", lambda h: h.tensor_tensor(
            out=s.r2[0:1, 0:T], in0=s.r0[0:1, 0:T], in1=s.r0[0:1, 0:T], op=ALU.mult))
        p.op("vector", lambda h: h.tensor_tensor(
            out=s.r1[0:1, 0:T], in0=s.r1[0:1, 0:T], in1=s.r2[0:1, 0:T],
            op=ALU.subtract), inc="dv")
    p.wait("scalar", "dv")
    p.op("scalar", lambda h: h.activation(out=s.r2[0:1, 0:T], in_=s.r1[0:1, 0:T],
                                          func=AF.Ln, bias=s.epsc[0:1, 0:1]))
    p.op("scalar", lambda h: h.activation(out=s.r3[0:1, 0:T], in_=s.r2[0:1, 0:T],
                                          func=AF.Exp, scale=-0.5), inc="sc")
    for th in range(nth):
        p.wait("tensor", "sc")
        p.wait("tensor", "dv")
        sl = slice(th * 512, th * 512 + 512)
        if not rms:
            p.op("tensor", lambda h, sl=sl: h.matmul(
                pa[:, 0:512], s.ones_c[0:1, :], s.r0[0:1, sl], start=True, stop=True))
        p.op("tensor", lambda h, sl=sl: h.matmul(
            pb[:, 0:512], s.ones_c[0:1, :], s.r3[0:1, sl], start=True, stop=True),
            inc="pe")
        p.wait("vector", "pe")
        if not rms:
            p.op("vector", lambda h, sl=sl: h.tensor_copy(out=s.mb[:, sl], in_=pa[:, 0:512]))
        p.op("vector", lambda h, sl=sl: h.tensor_copy(out=s.rb[:, sl], in_=pb[:, 0:512]),
             inc="dv")
    for c in range(nch):
        sl = slice(c * T, (c + 1) * T)
        if rms:
            p.op("vector", lambda h, sl=sl: h.tensor_tensor(
                out=s.xn[:, sl], in0=s.xb[:, sl], in1=s.rb[:, 0:T], op=ALU.mult),
                inc="dv" if c == nch - 1 else None)
        else:
            p.op("vector", lambda h, sl=sl: h.tensor_tensor(
                out=s.xn[:, sl], in0=s.xb[:, sl], in1=s.mb[:, 0:T], op=ALU.subtract))
            p.op("vector", lambda h, sl=sl: h.tensor_tensor(
                out=s.xn[:, sl], in0=s.xn[:, sl], in1=s.rb[:, 0:T], op=ALU.mult),
                inc="dv" if c == nch - 1 else None)


def emit_load_x_as_bf16(p, s, xT_ap, T, nch):
    conv_dv = {}
    for c in range(nch):
        slot = c % 2
        if c >= 2:
            p.wait("sync", "dv", conv_dv[c - 2])
        p.dma("sync", s.xtmp[:, slot * T:(slot + 1) * T],
              xT_ap[c * 128:(c + 1) * 128, :], f"ld{slot}")
        p.wait("vector", f"ld{slot}")
        p.op("vector", lambda h, c=c, slot=slot: h.tensor_copy(
            out=s.xb[:, c * T:(c + 1) * T], in_=s.xtmp[:, slot * T:(slot + 1) * T]),
            inc="dv")
        conv_dv[c] = p.semv["dv"]



def build_ghost(TT=1024, debug=False):
    D, I5, NCH, IC = 2048, 5504, 16, 43
    p = Prog("ghost")
    nc = p.nc

    xT = nc.dram_tensor("xT", [D, TT], F32, kind="ExternalInput").ap()
    gidxT = nc.dram_tensor("gidxT", [D, I5], U8, kind="ExternalInput").ap()
    uidxT = nc.dram_tensor("uidxT", [D, I5], U8, kind="ExternalInput").ap()
    didxT = nc.dram_tensor("didxT", [I5, D], U8, kind="ExternalInput").ap()
    lut = nc.dram_tensor("lut", [1, 16], F32, kind="ExternalInput").ap()
    yT = nc.dram_tensor("yT", [D, TT], F32, kind="ExternalOutput").ap()

    if debug:
        dbg_xn = nc.dram_tensor("dbg_xn", [128, 16 * TT], BF16, kind="ExternalOutput").ap()
        dbg_wg = nc.dram_tensor("dbg_wg", [256, I5], BF16, kind="ExternalOutput").ap()
        dbg_wd = nc.dram_tensor("dbg_wd", [256, D], BF16, kind="ExternalOutput").ap()
        dbg_h = nc.dram_tensor("dbg_h", [128, 43 * 512], BF16, kind="ExternalOutput").ap()
    WgS = nc.dram_tensor("WgS", [D, I5], BF16).ap()
    WuS = nc.dram_tensor("WuS", [D, I5], BF16).ap()
    WdS = nc.dram_tensor("WdS", [I5, D], BF16).ap()

    for sn in ("ld0", "ld1", "ldc", "wd0", "wd1", "wdc0", "wdc1", "wde0", "wde1",
               "cv", "dq", "so0", "so1", "pg", "ev", "dv", "pe", "sc", "yo0", "yo1"):
        p.sem(sn)

    s = NS()
    AB = 70 * 1024
    s.A = p.sbuf("A", [128, AB], BF16)
    s.xb = s.A[:, 0:NCH * TT]
    s.xn = s.A[:, NCH * TT:2 * NCH * TT]
    o_deq = 2 * NCH * TT
    idxb = [s.A[:, o_deq + k * I5: o_deq + (k + 1) * I5] for k in range(2)]
    accb = [s.A[:, o_deq + (2 + k) * I5: o_deq + (3 + k) * I5] for k in range(2)]
    tmpb = s.A[:, o_deq + 4 * I5: o_deq + 5 * I5]
    hb = s.A[:, o_deq: o_deq + IC * 512]
    o_w = o_deq + 5 * I5
    wg_l = [s.A[:, o_w + k * 2048: o_w + (k + 1) * 2048] for k in range(2)]
    wu_l = [s.A[:, o_w + (2 + k) * 2048: o_w + (3 + k) * 2048] for k in range(2)]
    o_wd = o_w + 4 * 2048
    wd_l = [s.A[:, o_wd + k * 512: o_wd + (k + 1) * 512] for k in range(2)]
    assert o_wd + 2 * 512 <= AB

    idx_u8 = p.sbuf("idx_u8", [128, 2 * I5], U8)
    s.xtmp = p.sbuf("xtmp", [128, 2 * TT], F32)
    s.ones_l = p.sbuf("ones_l", [128, 1], BF16)
    s.ones_c = p.sbuf("ones_c", [1, 128], F32)
    s.r0 = p.sbuf("r0", [1, TT], F32)
    s.r1 = p.sbuf("r1", [1, TT], F32)
    s.r2 = p.sbuf("r2", [1, TT], F32)
    s.r3 = p.sbuf("r3", [1, TT], F32)
    s.mb = p.sbuf("mb", [128, TT], F32)
    s.rb = p.sbuf("rb", [128, TT], F32)
    s.epsc = p.sbuf("epsc", [128, 1], F32)
    lutr = p.sbuf("lutr", [1, 16], F32)
    lut128 = p.sbuf("lut128", [128, 16], F32)
    gsil = p.sbuf("gsil", [128, 512], F32)
    xq = p.sbuf("xq", [128, 2 * 512], F32)
    oute = p.sbuf("oute", [128, 2 * 512], F32)

    P = [p.psum(f"P{i}", [128, 512]) for i in range(8)]
    pa, pb = P[0], P[1]

    p.dma("sync", lutr[:], lut, "ldc")
    p.op("vector", lambda h: h.memset(s.ones_l[:], 1.0))
    p.op("vector", lambda h: h.memset(s.epsc[:], 1e-5))
    p.op("vector", lambda h: h.memset(s.ones_c[:], 1.0), inc="dv")
    p.wait("tensor", "ldc")
    p.wait("tensor", "dv")
    p.op("tensor", lambda h: h.matmul(pa[:, 0:16], s.ones_c[0:1, :], lutr[0:1, :],
                                      start=True, stop=True), inc="pe")
    p.wait("vector", "pe")
    p.op("vector", lambda h: h.tensor_copy(out=lut128[:], in_=pa[:, 0:16]), inc="dv")
    emit_load_x_as_bf16(p, s, xT, TT, NCH)

    emit_ln(p, s, pa, pb, TT, NCH)

    jobs = []
    for i in range(16):
        jobs.append((gidxT[i * 128:(i + 1) * 128, :], I5, WgS[i * 128:(i + 1) * 128, :]))
    for i in range(16):
        jobs.append((uidxT[i * 128:(i + 1) * 128, :], I5, WuS[i * 128:(i + 1) * 128, :]))
    for i in range(IC):
        jobs.append((didxT[i * 128:(i + 1) * 128, :], D, WdS[i * 128:(i + 1) * 128, :]))
    n_jobs = len(jobs)
    for sidx, (src, cols, dst) in enumerate(jobs):
        sl = sidx % 2
        if sidx >= 2:
            p.wait("sync", "dq", sidx - 1)
        p.dma("sync", idx_u8[:, sl * I5: sl * I5 + cols], src, f"wdc{sl}")
        p.wait("gpsimd", f"wdc{sl}")
        p.op("gpsimd", lambda h, sl=sl, cols=cols: h.tensor_copy(
            out=idxb[sl][:, 0:cols], in_=idx_u8[:, sl * I5: sl * I5 + cols]), inc="cv")
        p.wait("vector", "cv")
        if sidx >= 2:
            p.wait("vector", f"so{sl}")
        p.op("vector", lambda h, sl=sl, cols=cols: h.tensor_scalar(
            out=accb[sl][:, 0:cols], in0=idxb[sl][:, 0:cols], scalar1=0.0,
            scalar2=lut128[:, 0:1], op0=ALU.is_equal, op1=ALU.mult))
        for v in range(1, 16):
            p.op("vector", lambda h, sl=sl, cols=cols, v=v: h.tensor_scalar(
                out=tmpb[:, 0:cols], in0=idxb[sl][:, 0:cols], scalar1=float(v),
                scalar2=lut128[:, v:v + 1], op0=ALU.is_equal, op1=ALU.mult))
            p.op("vector", lambda h, sl=sl, cols=cols: h.tensor_tensor(
                out=accb[sl][:, 0:cols], in0=accb[sl][:, 0:cols], in1=tmpb[:, 0:cols],
                op=ALU.add), inc="dq" if v == 15 else None)
        p.wait("gpsimd", "dq")
        p.dma("gpsimd", dst, accb[sl][:, 0:cols], f"so{sl}")

    if debug:
        p.wait("gpsimd", "dv")
        p.dma("gpsimd", dbg_xn, s.xn, "yo0")
        p.wait("gpsimd", "so", 75 * 16)
        p.dma("gpsimd", dbg_wg, WgS[0:256, :], "yo0")
        p.dma("gpsimd", dbg_wd, WdS[0:256, :], "yo0")

    guard = {}

    for th in range(2):
        t0 = th * 512
        if th == 1:
            p.wait("vector", "pg")
        p.wait("sync", "so0")
        p.wait("sync", "so1")
        for ic in range(IC):
            sl = ic % 2
            if ("wslot", sl) in guard:
                p.wait("sync", "pg", guard[("wslot", sl)])
            p.dma("sync", wg_l[sl].rearrange("p (k i) -> p k i", k=16),
                  WgS[:, ic * 128:(ic + 1) * 128].rearrange("(k p) i -> p k i", p=128),
                  f"wd{sl}")
            p.dma("sync", wu_l[sl].rearrange("p (k i) -> p k i", k=16),
                  WuS[:, ic * 128:(ic + 1) * 128].rearrange("(k p) i -> p k i", p=128),
                  f"wd{sl}")
            bg, bu = 2 + sl, 4 + sl
            p.wait("tensor", f"wd{sl}")
            p.wait("tensor", "dv")
            if ("evac", bg) in guard:
                p.wait("tensor", "ev", guard[("evac", bg)])
            for k in range(NCH):
                p.op("tensor", lambda h, k=k, sl=sl, bg=bg, t0=t0,
                     st=(k == 0), sp=(k == NCH - 1): h.matmul(
                         P[bg][:, 0:512], wg_l[sl][:, k * 128:(k + 1) * 128],
                         s.xn[:, k * TT + t0: k * TT + t0 + 512], start=st, stop=sp))
            for k in range(NCH):
                p.op("tensor", lambda h, k=k, sl=sl, bu=bu, t0=t0,
                     st=(k == 0), sp=(k == NCH - 1): h.matmul(
                         P[bu][:, 0:512], wu_l[sl][:, k * 128:(k + 1) * 128],
                         s.xn[:, k * TT + t0: k * TT + t0 + 512], start=st, stop=sp),
                     inc="pg" if k == NCH - 1 else None)
            guard[("wslot", sl)] = p.semv["pg"]
            p.wait("scalar", "pg")
            if "gsil" in guard:
                p.wait("scalar", "ev", guard["gsil"])
            p.op("scalar", lambda h, bg=bg: h.activation(
                out=gsil[:, 0:512], in_=P[bg][:, 0:512], func=AF.Silu), inc="sc")
            p.wait("vector", "sc")
            p.op("vector", lambda h, bu=bu, ic=ic: h.tensor_tensor(
                out=hb[:, ic * 512:(ic + 1) * 512], in0=gsil[:, 0:512],
                in1=P[bu][:, 0:512], op=ALU.mult), inc="ev")
            guard[("evac", bg)] = p.semv["ev"]
            guard[("evac", bu)] = p.semv["ev"]
            guard["gsil"] = p.semv["ev"]
        if debug and th == 0:
            p.wait("gpsimd", "ev")
            p.dma("gpsimd", dbg_h, hb, "yo0")
        p.wait("tensor", "ev")
        banks = [6, 7, 0, 1]
        for og in range(4):
            for b in banks:
                if ("evac", b) in guard:
                    p.wait("tensor", "ev", guard[("evac", b)])
            for ic in range(IC):
                sl = ic % 2
                if ("wdslot", sl) in guard:
                    p.wait("sync", "pg", guard[("wdslot", sl)])
                p.dma("sync", wd_l[sl],
                      WdS[ic * 128:(ic + 1) * 128, og * 512:(og + 1) * 512], f"wde{sl}")
                p.wait("tensor", f"wde{sl}")
                for j in range(4):
                    p.op("tensor", lambda h, b=banks[j], j=j, sl=sl, ic=ic: h.matmul(
                        P[b][:, 0:512], wd_l[sl][:, j * 128:(j + 1) * 128],
                        hb[:, ic * 512:(ic + 1) * 512],
                        start=(ic == 0), stop=(ic == IC - 1)),
                        inc="pg" if j == 3 else None)
                guard[("wdslot", sl)] = p.semv["pg"]
            chains_done = p.semv["pg"]
            for j in range(4):
                oc = og * 4 + j
                xs = slice((j % 2) * 512, (j % 2) * 512 + 512)
                if ("xq", j % 2) in guard:
                    p.wait("sync", "ev", guard[("xq", j % 2)])
                p.dma("sync", xq[:, xs], xT[oc * 128:(oc + 1) * 128, t0:t0 + 512],
                      f"ld{j % 2}")
                p.wait("vector", "pg", chains_done)
                p.wait("vector", f"ld{j % 2}")
                p.wait("vector", f"yo{j % 2}")
                p.op("vector", lambda h, b=banks[j], xs=xs: h.scalar_tensor_tensor(
                    out=oute[:, xs], in0=P[b][:, 0:512], scalar=0.5, in1=xq[:, xs],
                    op0=ALU.mult, op1=ALU.add), inc="ev")
                guard[("evac", banks[j])] = p.semv["ev"]
                guard[("xq", j % 2)] = p.semv["ev"]
                p.wait("gpsimd", "ev")
                p.dma("gpsimd", yT[oc * 128:(oc + 1) * 128, t0:t0 + 512], oute[:, xs],
                      f"yo{j % 2}")
    p.wait("gpsimd", "yo0")
    p.wait("gpsimd", "yo1")
    return p.build()


def stage_ghost_inputs(x_slice_T, gate_idx, up_idx, down_idx, lut):
    return {
        "xT": np.ascontiguousarray(x_slice_T, dtype=np.float32),
        "gidxT": np.ascontiguousarray(np.asarray(gate_idx).T).astype(np.uint8),
        "uidxT": np.ascontiguousarray(np.asarray(up_idx).T).astype(np.uint8),
        "didxT": np.ascontiguousarray(np.asarray(down_idx).T).astype(np.uint8),
        "lut": np.ascontiguousarray(np.asarray(lut), dtype=np.float32).reshape(1, 16),
    }



def build_mamba_front(TT=1024):
    D, I4, NCH = 2048, 4096, 16
    p = Prog("mfront")
    nc = p.nc
    xT = nc.dram_tensor("xT", [D, TT], F32, kind="ExternalInput").ap()
    wipT = nc.dram_tensor("wipT", [D, 2 * I4], F32, kind="ExternalInput").ap()
    hinT = nc.dram_tensor("hinT", [I4, TT], BF16, kind="ExternalOutput").ap()
    sgT = nc.dram_tensor("sgT", [I4, TT], BF16, kind="ExternalOutput").ap()

    for sn in ("ld0", "ld1", "wd0", "wd1", "cv", "dv", "pe", "sc", "ev", "pg",
               "yo0", "yo1", "yo2", "yo3"):
        p.sem(sn)

    s = NS()
    s.A = p.sbuf("A", [128, 40 * 1024], BF16)
    s.xb = s.A[:, 0:NCH * TT]
    s.xn = s.A[:, NCH * TT:2 * NCH * TT]
    wslab_b = [s.A[:, 32 * TT + k * 2048: 32 * TT + (k + 1) * 2048] for k in range(2)]
    outb = [s.A[:, 32 * TT + 2 * 2048 + k * 512: 32 * TT + 2 * 2048 + (k + 1) * 512]
            for k in range(4)]
    s.xtmp = p.sbuf("xtmp", [128, 2 * TT], F32)
    wslab_f = [p.sbuf(f"wsf{k}", [128, 2048], F32) for k in range(2)]
    s.ones_l = p.sbuf("ones_l", [128, 1], BF16)
    s.ones_c = p.sbuf("ones_c", [1, 128], F32)
    s.epsc = p.sbuf("epsc", [128, 1], F32)
    s.r0 = p.sbuf("r0", [1, TT], F32)
    s.r1 = p.sbuf("r1", [1, TT], F32)
    s.r2 = p.sbuf("r2", [1, TT], F32)
    s.r3 = p.sbuf("r3", [1, TT], F32)
    s.mb = p.sbuf("mb", [128, TT], F32)
    s.rb = p.sbuf("rb", [128, TT], F32)
    P = [p.psum(f"P{i}", [128, 512]) for i in range(6)]
    pa, pb = P[0], P[1]

    p.op("vector", lambda h: h.memset(s.ones_l[:], 1.0))
    p.op("vector", lambda h: h.memset(s.epsc[:], 1e-5))
    p.op("vector", lambda h: h.memset(s.ones_c[:], 1.0), inc="dv")
    emit_load_x_as_bf16(p, s, xT, TT, NCH)
    emit_ln(p, s, pa, pb, TT, NCH)

    guard = {}
    for oc in range(2 * I4 // 128):
        sl = oc % 2
        if ("wslot", sl) in guard:
            p.wait("sync", "cv", guard[("wslot", sl)])
        p.dma("sync", wslab_f[sl].rearrange("p (k i) -> p k i", k=16),
              wipT[:, oc * 128:(oc + 1) * 128].rearrange("(k p) i -> p k i", p=128),
              f"wd{sl}")
        p.wait("vector", f"wd{sl}")
        if ("wbslot", sl) in guard:
            p.wait("vector", "pg", guard[("wbslot", sl)])
        p.op("vector", lambda h, sl=sl: h.tensor_copy(out=wslab_b[sl], in_=wslab_f[sl][:]),
             inc="cv")
        guard[("wslot", sl)] = p.semv["cv"]
        p.wait("tensor", "cv")
        p.wait("tensor", "dv")
        for th in range(2):
            bk = 2 + 2 * (oc % 2) + th
            if ("evac", bk) in guard:
                p.wait("tensor", "ev", guard[("evac", bk)])
            for k in range(NCH):
                p.op("tensor", lambda h, k=k, sl=sl, bk=bk, th=th,
                     st=(k == 0), sp=(k == NCH - 1): h.matmul(
                         P[bk][:, 0:512], wslab_b[sl][:, k * 128:(k + 1) * 128],
                         s.xn[:, k * TT + th * 512: k * TT + th * 512 + 512],
                         start=st, stop=sp), inc="pg" if k == NCH - 1 else None)
        guard[("wbslot", sl)] = p.semv["pg"]
        p.wait("scalar", "pg")
        for th in range(2):
            bk = 2 + 2 * (oc % 2) + th
            slot = 2 * (oc % 2) + th
            ob = outb[slot]
            p.wait("scalar", f"yo{slot}")
            if oc < 32:
                p.op("scalar", lambda h, bk=bk, ob=ob: h.copy(out=ob, in_=P[bk][:, 0:512]),
                     inc="ev")
            else:
                p.op("scalar", lambda h, bk=bk, ob=ob: h.activation(
                    out=ob, in_=P[bk][:, 0:512], func=AF.Silu), inc="ev")
            guard[("evac", bk)] = p.semv["ev"]
            p.wait("gpsimd", "ev")
            dst = (hinT if oc < 32 else sgT)
            r0 = (oc if oc < 32 else oc - 32) * 128
            p.dma("gpsimd", dst[r0:r0 + 128, th * 512:th * 512 + 512], ob, f"yo{slot}")
    for sn in ("yo0", "yo1", "yo2", "yo3"):
        p.wait("gpsimd", sn)
    return p.build()



def build_conv(CH=512, T=8192):
    p = Prog("conv")
    nc = p.nc
    hinT = nc.dram_tensor("hinT", [CH, T], BF16, kind="ExternalInput").ap()
    convw = nc.dram_tensor("convw", [CH, 4], F32, kind="ExternalInput").ap()
    convb = nc.dram_tensor("convb", [CH, 1], F32, kind="ExternalInput").ap()
    uT = nc.dram_tensor("uT", [CH, T], BF16, kind="ExternalOutput").ap()
    for sn in ("ld0", "ld1", "ldc", "dv", "sc", "yo0", "yo1"):
        p.sem(sn)
    nic = CH // 128
    hsb = p.sbuf("hsb", [128, 2 * (T + 3)], BF16)
    acc = p.sbuf("acc", [128, T], F32)
    ub = p.sbuf("ub", [128, 2 * T], BF16)
    wc = p.sbuf("wc", [128, 4 * nic], F32)
    bc = p.sbuf("bc", [128, nic], F32)
    p.dma("sync", wc.rearrange("p (k f) -> p k f", k=nic),
          convw.rearrange("(k p) f -> p k f", p=128), "ldc")
    p.dma("sync", bc[:], convb.rearrange("(k p) f -> p (k f)", p=128), "ldc", slow=True)
    guard = {}
    for ic in range(nic):
        sl = ic % 2
        o = sl * (T + 3)
        if ("h", sl) in guard:
            p.wait("sync", "sc", guard[("h", sl)])
        p.dma("sync", hsb[:, o + 3: o + 3 + T], hinT[ic * 128:(ic + 1) * 128, :],
              f"ld{sl}")
        p.wait("vector", f"ld{sl}")
        p.wait("vector", "ldc")
        if ic == 0:
            p.op("vector", lambda h: h.memset(hsb[:, 0:3], 0.0))
            p.op("vector", lambda h: h.memset(hsb[:, T + 3: T + 6], 0.0))
        if ic >= 1:
            p.wait("vector", "sc", guard[("acc", 0)])
        p.op("vector", lambda h, o=o, ic=ic: h.tensor_scalar(
            out=acc[:], in0=hsb[:, o: o + T], scalar1=wc[:, 4 * ic: 4 * ic + 1],
            scalar2=None, op0=ALU.mult))
        for k in range(1, 4):
            p.op("vector", lambda h, o=o, ic=ic, k=k: h.scalar_tensor_tensor(
                out=acc[:], in0=hsb[:, o + k: o + k + T],
                scalar=wc[:, 4 * ic + k: 4 * ic + k + 1], in1=acc[:],
                op0=ALU.mult, op1=ALU.add), inc="dv" if k == 3 else None)
        guard[("h", sl)] = p.semv["dv"]
        p.wait("scalar", "dv")
        p.wait("scalar", f"yo{sl}")
        p.op("scalar", lambda h, sl=sl, ic=ic: h.activation(
            out=ub[:, sl * T:(sl + 1) * T], in_=acc[:], func=AF.Silu,
            bias=bc[:, ic:ic + 1]), inc="sc")
        guard[("h", sl)] = p.semv["sc"]
        guard[("acc", 0)] = p.semv["sc"]
        p.wait("gpsimd", "sc")
        p.dma("gpsimd", uT[ic * 128:(ic + 1) * 128, :], ub[:, sl * T:(sl + 1) * T],
              f"yo{sl}")
    p.wait("gpsimd", "yo0")
    p.wait("gpsimd", "yo1")
    return p.build()



def build_xproj_dt(TT=1024, do_dt=True, do_bc=True, dt_mode='full'):
    I4, R = 4096, 128
    p = Prog("xproj")
    nc = p.nc
    uT = nc.dram_tensor("uT", [I4, TT], BF16, kind="ExternalInput").ap()
    xpT = nc.dram_tensor("xpT", [I4, 160], F32, kind="ExternalInput").ap()
    dtpT = nc.dram_tensor("dtpT", [R, I4], F32, kind="ExternalInput").ap()
    dtb = nc.dram_tensor("dtb", [I4, 1], F32, kind="ExternalInput").ap()
    dtT = nc.dram_tensor("dtT", [I4, TT], F32, kind="ExternalOutput").ap()
    Bo = nc.dram_tensor("Bo", [16, TT], F32, kind="ExternalOutput").ap()
    Co = nc.dram_tensor("Co", [16, TT], F32, kind="ExternalOutput").ap()
    for sn in ("ld", "cv", "dv", "pe", "sc", "ev", "pg", "yo0", "yo1", "yob"):
        p.sem(sn)
    nk = I4 // 128
    ub = p.sbuf("ub", [128, nk * TT], BF16)
    xpf = p.sbuf("xpf", [128, nk * 160], F32)
    xpb = p.sbuf("xpb", [128, nk * 160], BF16)
    dtpf = p.sbuf("dtpf", [128, I4], F32)
    dtpb = p.sbuf("dtpb", [128, I4], BF16)
    dtbc = p.sbuf("dtbc", [128, nk], F32)
    tsb = p.sbuf("tsb", [128, TT], BF16)
    t1 = p.sbuf("t1", [128, 512], F32)
    a1 = p.sbuf("a1", [128, 512], F32)
    dto = p.sbuf("dto", [128, 2 * 512], F32)
    bco = p.sbuf("bco", [32, TT], F32)
    P = [p.psum(f"P{i}", [128, 512]) for i in range(6)]

    p.dma("sync", ub.rearrange("p (k t) -> p k t", k=nk),
          uT.rearrange("(k p) t -> p k t", p=128), "ld")
    p.dma("sync", xpf.rearrange("p (k r) -> p k r", k=nk),
          xpT.rearrange("(k p) r -> p k r", p=128), "ld")
    p.dma("sync", dtpf[:], dtpT, "ld")
    p.dma("sync", dtbc[:], dtb.rearrange("(k p) f -> p (k f)", p=128), "ld", slow=True)
    p.wait("vector", "ld")
    p.op("vector", lambda h: h.tensor_copy(out=xpb[:], in_=xpf[:]))
    p.op("vector", lambda h: h.tensor_copy(out=dtpb[:], in_=dtpf[:]), inc="cv")
    p.wait("tensor", "cv")
    for th in range(2):
        for k in range(nk):
            p.op("tensor", lambda h, k=k, th=th, st=(k == 0), sp=(k == nk - 1): h.matmul(
                P[0 + th][:, 0:512], xpb[:, k * 160: k * 160 + 128],
                ub[:, k * TT + th * 512: k * TT + th * 512 + 512], start=st, stop=sp))
            p.op("tensor", lambda h, k=k, th=th, st=(k == 0), sp=(k == nk - 1): h.matmul(
                P[2 + th][0:32, 0:512], xpb[:, k * 160 + 128: (k + 1) * 160],
                ub[:, k * TT + th * 512: k * TT + th * 512 + 512], start=st, stop=sp),
                inc="pg" if k == nk - 1 else None)
        p.wait("vector", "pg")
        p.op("vector", lambda h, th=th: h.tensor_copy(
            out=tsb[:, th * 512:(th + 1) * 512], in_=P[0 + th][:, 0:512]))
        p.op("vector", lambda h, th=th: h.tensor_copy(
            out=bco[0:32, th * 512:(th + 1) * 512], in_=P[2 + th][0:32, 0:512]),
            inc="dv")
    p.wait("gpsimd", "dv")
    if do_bc:
        p.dma("gpsimd", Bo, bco[0:16, :], "yob")
        p.dma("gpsimd", Co, bco[16:32, :], "yob")
    else:
        p.dma("gpsimd", Bo, bco[0:16, :], "yob")
        p.dma("gpsimd", Co, bco[0:16, :], "yob")
    guard = {}
    for oc in range(nk if do_dt else 0):
        for th in range(2):
            bk = 4 + (oc * 2 + th) % 2
            if ("e", bk) in guard:
                p.wait("tensor", "ev", guard[("e", bk)])
            if ("ed", bk) in guard:
                p.wait("tensor", "dv", guard[("ed", bk)])
            else:
                p.wait("tensor", "dv")
            p.op("tensor", lambda h, oc=oc, th=th, bk=bk: h.matmul(
                P[bk][:, 0:512], dtpb[:, oc * 128:(oc + 1) * 128],
                tsb[:, th * 512:(th + 1) * 512], start=True, stop=True), inc="pg")
            v = p.semv["pg"]
            p.wait("vector", "pg", v)
            p.wait("scalar", "pg", v)
            p.wait("vector", f"yo{th}")
            p.wait("scalar", f"yo{th}")
            if dt_mode in ('abs', 'absexp', 'absexpln'):
                p.wait("scalar", "dv")
                p.op("scalar", lambda h, bk=bk, oc=oc: h.activation(
                    out=a1[:], in_=P[bk][:, 0:512], func=AF.Abs, bias=dtbc[:, oc:oc + 1]),
                    inc="ev")
                guard[("e", bk)] = p.semv["ev"]
                if dt_mode in ('absexp', 'absexpln'):
                    p.op("scalar", lambda h: h.activation(out=a1[:], in_=a1[:],
                                                          func=AF.Exp, scale=-1.0),
                         inc="sc")
                if dt_mode == 'absexpln':
                    p.op("scalar", lambda h: h.activation(out=a1[:], in_=a1[:],
                                                          func=AF.Ln, bias=1.0),
                         inc="sc")
                p.op("vector", lambda h, th=th: h.tensor_copy(
                    out=dto[:, th * 512:(th + 1) * 512], in_=a1[:]), inc="dv")
            elif dt_mode == 'copy':
                p.wait("scalar", "dv")
                p.op("scalar", lambda h, bk=bk: h.copy(out=a1[:], in_=P[bk][:, 0:512]),
                     inc="ev")
                guard[("e", bk)] = p.semv["ev"]
                p.op("vector", lambda h, th=th: h.tensor_copy(
                    out=dto[:, th * 512:(th + 1) * 512], in_=a1[:]), inc="dv")
            elif dt_mode == 'ts':
                p.op("vector", lambda h, bk=bk, oc=oc: h.tensor_scalar(
                    out=t1[:], in0=P[bk][:, 0:512], scalar1=dtbc[:, oc:oc + 1],
                    scalar2=0.0, op0=ALU.add, op1=ALU.max), inc="ev")
                guard[("e", bk)] = p.semv["ev"]
                p.op("vector", lambda h, th=th: h.tensor_copy(
                    out=dto[:, th * 512:(th + 1) * 512], in_=t1[:]), inc="dv")
            else:
                p.op("vector", lambda h, bk=bk, oc=oc: h.tensor_scalar(
                    out=t1[:], in0=P[bk][:, 0:512], scalar1=dtbc[:, oc:oc + 1],
                    scalar2=0.0, op0=ALU.add, op1=ALU.max), inc="dv")
                p.wait("scalar", "dv")
                p.op("scalar", lambda h, bk=bk, oc=oc: h.activation(
                    out=a1[:], in_=P[bk][:, 0:512], func=AF.Abs, bias=dtbc[:, oc:oc + 1]),
                    inc="ev")
                guard[("e", bk)] = p.semv["ev"]
                p.op("scalar", lambda h: h.activation(out=a1[:], in_=a1[:], func=AF.Exp,
                                                      scale=-1.0))
                p.op("scalar", lambda h: h.activation(out=a1[:], in_=a1[:], func=AF.Ln,
                                                      bias=1.0), inc="sc")
                p.wait("vector", "sc")
                p.op("vector", lambda h, th=th: h.tensor_tensor(
                    out=dto[:, th * 512:(th + 1) * 512], in0=t1[:], in1=a1[:], op=ALU.add),
                    inc="dv")
            guard[("ed", bk)] = p.semv["dv"]
            p.wait("gpsimd", "dv")
            p.dma("gpsimd", dtT[oc * 128:(oc + 1) * 128, th * 512:(th + 1) * 512],
                  dto[:, th * 512:(th + 1) * 512], f"yo{th}")
    if not do_dt:
        p.wait("gpsimd", "ld")
        p.op("gpsimd", lambda h: h.memset(dto[:, 0:4], 0.0))
        p.dma("gpsimd", dtT[0:128, 0:4], dto[:, 0:4], "yo0")
    p.wait("gpsimd", "yo0")
    p.wait("gpsimd", "yo1")
    p.wait("gpsimd", "yob")
    return p.build()



def build_scan(CH=512, T=8192, TC=2048):
    N = 16
    p = Prog("scan")
    nc = p.nc
    uT = nc.dram_tensor("uT", [CH, T], BF16, kind="ExternalInput").ap()
    sgT = nc.dram_tensor("sgT", [CH, T], BF16, kind="ExternalInput").ap()
    dtT = nc.dram_tensor("dtT", [CH, T], F32, kind="ExternalInput").ap()
    Bf = nc.dram_tensor("Bf", [16, T], F32, kind="ExternalInput").ap()
    Cf = nc.dram_tensor("Cf", [16, T], F32, kind="ExternalInput").ap()
    Alog = nc.dram_tensor("Alog", [CH, N], F32, kind="ExternalInput").ap()
    sel = nc.dram_tensor("sel", [16, 16 * 128], BF16, kind="ExternalInput").ap()
    Dv = nc.dram_tensor("Dv", [CH, 1], F32, kind="ExternalInput").ap()
    ygT = nc.dram_tensor("ygT", [CH, T], BF16, kind="ExternalOutput").ap()

    for sn in ("ld", "ldc", "wd", "dv", "pe", "sc", "ev", "pg", "yo0", "yo1"):
        p.sem(sn)
    nic = CH // 128
    ntc = T // TC
    dt_t = [p.sbuf(f"dt{ic}", [128, TC], F32) for ic in range(nic)]
    u_t = [p.sbuf(f"u{ic}", [128, TC], BF16) for ic in range(nic)]
    sg_t = [p.sbuf(f"sg{ic}", [128, TC], BF16) for ic in range(nic)]
    dtu_t = [p.sbuf(f"dtu{ic}", [128, TC], F32) for ic in range(nic)]
    y_t = [p.sbuf(f"y{ic}", [128, TC], F32) for ic in range(nic)]
    dA = p.sbuf("dA", [128, TC], F32)
    dBx = p.sbuf("dBx", [128, TC], F32)
    hh = p.sbuf("hh", [128, TC], F32)
    Bb = p.sbuf("Bb", [128, TC], F32)
    Cb = p.sbuf("Cb", [128, TC], F32)
    Brow = p.sbuf("Brow", [16, TC], BF16)
    Crow = p.sbuf("Crow", [16, TC], BF16)
    Brf = p.sbuf("Brf", [16, TC], F32)
    Crf = p.sbuf("Crf", [16, TC], F32)
    Acol = p.sbuf("Acol", [128, nic * N], F32)
    Dcol = p.sbuf("Dcol", [128, nic], F32)
    state = p.sbuf("state", [128, nic * N], F32)
    ygb = p.sbuf("ygb", [128, 2 * TC], BF16)
    ones_c = p.sbuf("ones_c", [1, 128], BF16)
    selt = p.sbuf("selt", [16, 16 * 128], BF16)
    pbc = p.psum("pbc", [128, 512])
    pbc2 = p.psum("pbc2", [128, 512])

    p.dma("sync", selt[:], sel, "ldc")
    p.dma("sync", Acol.rearrange("p (k n) -> p k n", k=nic),
          Alog.rearrange("(k p) n -> p k n", p=128), "ldc")
    p.dma("sync", Dcol[:], Dv.rearrange("(k p) f -> p (k f)", p=128), "ldc", slow=True)
    p.wait("vector", "ldc")
    p.op("vector", lambda h: h.memset(ones_c[:], 1.0))
    p.op("vector", lambda h: h.memset(state[:], 0.0), inc="dv")
    p.wait("scalar", "ldc")
    p.op("scalar", lambda h: h.activation(out=Acol[:], in_=Acol[:], func=AF.Exp),
         inc="sc")
    p.wait("vector", "sc")
    p.op("vector", lambda h: h.tensor_scalar(
        out=Acol[:], in0=Acol[:], scalar1=-1.0, scalar2=None, op0=ALU.mult), inc="dv")

    guard = {("dAfree",): p.semv["dv"]}
    for tc in range(ntc):
        t0 = tc * TC
        if tc > 0:
            p.wait("sync", "dv", guard[("bcrows",)])
        p.dma("sync", Brf[:], Bf[:, t0:t0 + TC], "wd")
        p.dma("sync", Crf[:], Cf[:, t0:t0 + TC], "wd")
        p.wait("vector", "wd")
        if tc > 0:
            p.wait("vector", "dv", guard[("bcrows",)])
        p.op("vector", lambda h: h.tensor_copy(out=Brow[:], in_=Brf[:]))
        p.op("vector", lambda h: h.tensor_copy(out=Crow[:], in_=Crf[:]), inc="dv")
        guard[("bcrows",)] = p.semv["dv"]
        for ic in range(nic):
            if tc > 0:
                p.wait("sync", "dv", guard[("done", ic)])
            p.dma("sync", dt_t[ic][:], dtT[ic * 128:(ic + 1) * 128, t0:t0 + TC], "ld")
            p.dma("sync", u_t[ic][:], uT[ic * 128:(ic + 1) * 128, t0:t0 + TC], "ld")
            p.dma("sync", sg_t[ic][:], sgT[ic * 128:(ic + 1) * 128, t0:t0 + TC], "ld")
        p.wait("vector", "ld")
        p.wait("scalar", "ld")
        for ic in range(nic):
            p.op("vector", lambda h, ic=ic: h.tensor_scalar(
                out=y_t[ic][:], in0=u_t[ic][:], scalar1=Dcol[:, ic:ic + 1],
                scalar2=None, op0=ALU.mult))
            p.op("vector", lambda h, ic=ic: h.tensor_tensor(
                out=dtu_t[ic][:], in0=dt_t[ic][:], in1=u_t[ic][:], op=ALU.mult),
                inc="dv")
        for n in range(N):
            nbt = TC // 512
            for j in range(nbt):
                p.wait("tensor", "dv")
                if ("bb", 0) in guard:
                    p.wait("tensor", "dv", guard[("bb", 0)])
                p.op("tensor", lambda h, n=n, j=j: h.matmul(
                    pbc[:, 0:512], selt[:, n * 128:(n + 1) * 128],
                    Brow[:, j * 512: (j + 1) * 512],
                    start=True, stop=True))
                p.op("tensor", lambda h, n=n, j=j: h.matmul(
                    pbc2[:, 0:512], selt[:, n * 128:(n + 1) * 128],
                    Crow[:, j * 512: (j + 1) * 512],
                    start=True, stop=True), inc="pe")
                p.wait("vector", "pe")
                p.op("vector", lambda h, j=j: h.tensor_copy(
                    out=Bb[:, j * 512:(j + 1) * 512], in_=pbc[:, 0:512]))
                p.op("vector", lambda h, j=j: h.tensor_copy(
                    out=Cb[:, j * 512:(j + 1) * 512], in_=pbc2[:, 0:512]), inc="dv")
            guard[("bb", 0)] = p.semv["dv"]
            for ic in range(nic):
                p.wait("scalar", "dv", guard.get(("dAfree",), 0))
                p.op("scalar", lambda h, ic=ic, n=n: h.activation(
                    out=dA[:], in_=dt_t[ic][:], func=AF.Exp,
                    scale=Acol[:, ic * N + n: ic * N + n + 1]), inc="sc")
                p.wait("vector", "sc")
                p.op("vector", lambda h, ic=ic: h.tensor_tensor(
                    out=dBx[:], in0=dtu_t[ic][:], in1=Bb[:], op=ALU.mult))
                p.op("vector", lambda h, ic=ic, n=n: h.tensor_tensor_scan(
                    out=hh[:], data0=dA[:], data1=dBx[:],
                    initial=state[:, ic * N + n: ic * N + n + 1],
                    op0=ALU.mult, op1=ALU.add))
                guard[("dAfree",)] = p.semv["dv"] + 2
                p.op("vector", lambda h, ic=ic, n=n: h.tensor_copy(
                    out=state[:, ic * N + n: ic * N + n + 1], in_=hh[:, TC - 1: TC]))
                p.op("vector", lambda h, ic=ic: h.tensor_tensor(
                    out=hh[:], in0=hh[:], in1=Cb[:], op=ALU.mult))
                p.op("vector", lambda h, ic=ic: h.tensor_tensor(
                    out=y_t[ic][:], in0=y_t[ic][:], in1=hh[:], op=ALU.add), inc="dv")
                guard[("dAfree",)] = p.semv["dv"]
        for ic in range(nic):
            sl = ic % 2
            p.wait("vector", f"yo{sl}")
            p.op("vector", lambda h, ic=ic, sl=sl: h.tensor_tensor(
                out=ygb[:, sl * TC:(sl + 1) * TC], in0=y_t[ic][:], in1=sg_t[ic][:],
                op=ALU.mult), inc="dv")
            guard[("done", ic)] = p.semv["dv"]
            p.wait("gpsimd", "dv")
            p.dma("gpsimd", ygT[ic * 128:(ic + 1) * 128, t0:t0 + TC],
                  ygb[:, sl * TC:(sl + 1) * TC], f"yo{sl}")
    p.wait("gpsimd", "yo0")
    p.wait("gpsimd", "yo1")
    return p.build()



def build_outproj(TT=1024, k_dim=4096):
    I4, D = k_dim, 2048
    p = Prog("outp")
    nc = p.nc
    ygT = nc.dram_tensor("ygT", [I4, TT], BF16, kind="ExternalInput").ap()
    wopT = nc.dram_tensor("wopT", [I4, D], F32, kind="ExternalInput").ap()
    xT = nc.dram_tensor("xT", [D, TT], F32, kind="ExternalInput").ap()
    yT = nc.dram_tensor("yT", [D, TT], F32, kind="ExternalOutput").ap()
    for sn in ("ldy", "ld0", "ld1", "wd0", "wd1", "cv", "dv", "pe", "ev", "pg",
               "yo0", "yo1"):
        p.sem(sn)
    nk = I4 // 128
    ygb = p.sbuf("ygb", [128, nk * TT], BF16)
    wf = [p.sbuf(f"wf{k}", [128, nk * 128], F32) for k in range(2)]
    wb = [p.sbuf(f"wb{k}", [128, nk * 128], BF16) for k in range(2)]
    xq = p.sbuf("xq", [128, 2 * 512], F32)
    oute = p.sbuf("oute", [128, 2 * 512], F32)
    P = [p.psum(f"P{i}", [128, 512]) for i in range(4)]

    p.dma("sync", ygb.rearrange("p (k t) -> p k t", k=nk),
          ygT.rearrange("(k p) t -> p k t", p=128), "ldy")
    guard = {}
    for oc in range(D // 128):
        sl = oc % 2
        if ("w", sl) in guard:
            p.wait("sync", "cv", guard[("w", sl)])
        p.dma("sync", wf[sl].rearrange("p (k i) -> p k i", k=nk),
              wopT[:, oc * 128:(oc + 1) * 128].rearrange("(k p) i -> p k i", p=128),
              f"wd{sl}")
        p.wait("vector", f"wd{sl}")
        if ("wb", sl) in guard:
            p.wait("vector", "pg", guard[("wb", sl)])
        p.op("vector", lambda h, sl=sl: h.tensor_copy(out=wb[sl][:], in_=wf[sl][:]),
             inc="cv")
        guard[("w", sl)] = p.semv["cv"]
        p.wait("tensor", "cv")
        p.wait("tensor", "ldy")
        for th in range(2):
            bk = 2 * sl + th
            if ("e", bk) in guard:
                p.wait("tensor", "ev", guard[("e", bk)])
            for k in range(nk):
                p.op("tensor", lambda h, k=k, th=th, sl=sl, bk=bk,
                     st=(k == 0), sp=(k == nk - 1): h.matmul(
                         P[bk][:, 0:512], wb[sl][:, k * 128:(k + 1) * 128],
                         ygb[:, k * TT + th * 512: k * TT + th * 512 + 512],
                         start=st, stop=sp), inc="pg" if k == nk - 1 else None)
        guard[("wb", sl)] = p.semv["pg"]
        for th in range(2):
            bk = 2 * sl + th
            xs = slice(th * 512, (th + 1) * 512)
            if ("xq", th) in guard:
                p.wait("sync", "ev", guard[("xq", th)])
            p.dma("sync", xq[:, xs], xT[oc * 128:(oc + 1) * 128, th * 512:(th + 1) * 512],
                  f"ld{th}")
            p.wait("vector", "pg")
            p.wait("vector", f"ld{th}")
            p.wait("vector", f"yo{th}")
            p.op("vector", lambda h, bk=bk, xs=xs: h.tensor_tensor(
                out=oute[:, xs], in0=P[bk][:, 0:512], in1=xq[:, xs], op=ALU.add),
                inc="ev")
            guard[("e", bk)] = p.semv["ev"]
            guard[("xq", th)] = p.semv["ev"]
            p.wait("gpsimd", "ev")
            p.dma("gpsimd", yT[oc * 128:(oc + 1) * 128, th * 512:(th + 1) * 512],
                  oute[:, xs], f"yo{th}")
    p.wait("gpsimd", "yo0")
    p.wait("gpsimd", "yo1")
    return p.build()


def make_sel():
    sel = np.zeros((16, 16 * 128), dtype=np.float32)
    for n in range(16):
        sel[n, n * 128:(n + 1) * 128] = 1.0
    return np_bf16(sel)



def build_l1ffn(TT=1024):
    D, I8, NCH = 2048, 8192, 16
    NIC = I8 // 128
    p = Prog("l1ffn")
    nc = p.nc
    xT = nc.dram_tensor("xT", [D, TT], F32, kind="ExternalInput").ap()
    gwT = nc.dram_tensor("gwT", [D, I8], U8, kind="ExternalInput").ap()
    uwT = nc.dram_tensor("uwT", [D, I8], U8, kind="ExternalInput").ap()
    dwT = nc.dram_tensor("dwT", [I8, D], U8, kind="ExternalInput").ap()
    gsc = nc.dram_tensor("gsc", [I8, 1], F32, kind="ExternalInput").ap()
    usc = nc.dram_tensor("usc", [I8, 1], F32, kind="ExternalInput").ap()
    dsc = nc.dram_tensor("dsc", [D, 1], F32, kind="ExternalInput").ap()
    yT = nc.dram_tensor("yT", [D, TT], F32, kind="ExternalOutput").ap()

    for sn in ("ld0", "ld1", "ldc", "wd0", "wd1", "wde0", "wde1", "cv", "dv", "pe",
               "sc", "ev", "pg", "yo0", "yo1", "hs"):
        p.sem(sn)

    s = NS()
    AB = 74 * 1024
    s.A = p.sbuf("A", [128, AB], BF16)
    s.xb = s.A[:, 0:NCH * TT]
    s.xn = s.A[:, NCH * TT:2 * NCH * TT]
    hb = s.A[:, 2 * NCH * TT: 2 * NCH * TT + NIC * 512]
    o_w = 2 * NCH * TT + NIC * 512
    wgb = [s.A[:, o_w + k * 2048: o_w + (k + 1) * 2048] for k in range(2)]
    wub = [s.A[:, o_w + (2 + k) * 2048: o_w + (3 + k) * 2048] for k in range(2)]
    o_wd = o_w + 4 * 2048
    wdb = [s.A[:, o_wd + k * 512: o_wd + (k + 1) * 512] for k in range(2)]
    assert o_wd + 1024 <= AB

    wg8 = [p.sbuf(f"wg8{k}", [128, 2048], U8) for k in range(2)]
    wu8 = [p.sbuf(f"wu8{k}", [128, 2048], U8) for k in range(2)]
    wd8 = [p.sbuf(f"wd8{k}", [128, 512], U8) for k in range(2)]
    s.xtmp = p.sbuf("xtmp", [128, 2 * TT], F32)
    s.ones_l = p.sbuf("ones_l", [128, 1], BF16)
    s.ones_c = p.sbuf("ones_c", [1, 128], F32)
    s.epsc = p.sbuf("epsc", [128, 1], F32)
    s.r0 = p.sbuf("r0", [1, TT], F32)
    s.r1 = p.sbuf("r1", [1, TT], F32)
    s.r2 = p.sbuf("r2", [1, TT], F32)
    s.r3 = p.sbuf("r3", [1, TT], F32)
    s.mb = p.sbuf("mb", [128, TT], F32)
    s.rb = p.sbuf("rb", [128, TT], F32)
    m128 = p.sbuf("m128", [1, 128], BF16)
    xsb = p.sbuf("xsb", [1, TT], BF16)
    hsb = p.sbuf("hsb", [1, TT], BF16)
    gcol = p.sbuf("gcol", [128, NIC], F32)
    ucol = p.sbuf("ucol", [128, NIC], F32)
    dcol = p.sbuf("dcol", [128, NCH], F32)
    dcolh = p.sbuf("dcolh", [128, NCH], F32)
    gs = p.sbuf("gs", [128, 512], F32)
    xq = p.sbuf("xq", [128, 2 * 512], F32)
    oute = p.sbuf("oute", [128, 2 * 512], F32)
    P = [p.psum(f"P{i}", [128, 512]) for i in range(8)]
    pa, pb = P[0], P[1]

    p.op("vector", lambda h: h.memset(s.ones_l[:], 1.0))
    p.op("vector", lambda h: h.memset(s.epsc[:], 1e-5))
    p.op("vector", lambda h: h.memset(m128[:], -128.0))
    p.op("vector", lambda h: h.memset(s.ones_c[:], 1.0), inc="dv")
    p.dma("sync", gcol[:], gsc.rearrange("(k p) f -> p (k f)", p=128), "ldc", slow=True)
    p.dma("sync", ucol[:], usc.rearrange("(k p) f -> p (k f)", p=128), "ldc", slow=True)
    p.dma("sync", dcol[:], dsc.rearrange("(k p) f -> p (k f)", p=128), "ldc", slow=True)
    p.wait("vector", "ldc")
    p.wait("scalar", "ldc")
    p.op("vector", lambda h: h.tensor_scalar(
        out=dcolh[:], in0=dcol[:], scalar1=0.5, scalar2=None, op0=ALU.mult), inc="dv")
    emit_load_x_as_bf16(p, s, xT, TT, NCH)
    emit_ln(p, s, pa, pb, TT, NCH)

    for th in range(2):
        p.wait("tensor", "dv")
        for c in range(NCH):
            k0 = c * TT + th * 512
            p.op("tensor", lambda h, k0=k0, st=(c == 0), sp=(c == NCH - 1): h.matmul(
                pa[0:1, :], s.ones_l[:, 0:1], s.xn[:, k0:k0 + 512], start=st, stop=sp),
                inc="pe" if c == NCH - 1 else None)
        p.wait("vector", "pe")
        p.op("vector", lambda h, th=th: h.tensor_copy(
            out=xsb[0:1, th * 512:(th + 1) * 512], in_=pa[0:1, :]), inc="dv")

    guard = {}
    for th in range(2):
        t0 = th * 512
        if th == 1:
            p.wait("vector", "pg")
        for ic in range(NIC):
            sl = ic % 2
            if ("w8", sl) in guard:
                p.wait("sync", "cv", guard[("w8", sl)])
            p.dma("sync", wg8[sl][:].rearrange("p (k i) -> p k i", k=16),
                  gwT[:, ic * 128:(ic + 1) * 128].rearrange("(k p) i -> p k i", p=128),
                  f"wd{sl}")
            p.dma("sync", wu8[sl][:].rearrange("p (k i) -> p k i", k=16),
                  uwT[:, ic * 128:(ic + 1) * 128].rearrange("(k p) i -> p k i", p=128),
                  f"wd{sl}")
            p.wait("gpsimd", f"wd{sl}")
            if ("wb", sl) in guard:
                p.wait("gpsimd", "pg", guard[("wb", sl)])
            p.op("gpsimd", lambda h, sl=sl: h.tensor_copy(out=wgb[sl], in_=wg8[sl][:]))
            p.op("gpsimd", lambda h, sl=sl: h.tensor_copy(out=wub[sl], in_=wu8[sl][:]),
                 inc="cv")
            guard[("w8", sl)] = p.semv["cv"]
            bg, bu = 2 + sl, 4 + sl
            p.wait("tensor", "cv")
            p.wait("tensor", "dv")
            for (bk, wb) in ((bg, wgb[sl]), (bu, wub[sl])):
                if ("evac", bk) in guard:
                    p.wait("tensor", "ev", guard[("evac", bk)])
                for k in range(NCH):
                    p.op("tensor", lambda h, k=k, bk=bk, wb=wb, t0=t0,
                         st=(k == 0): h.matmul(
                             P[bk][:, 0:512], wb[:, k * 128:(k + 1) * 128],
                             s.xn[:, k * TT + t0: k * TT + t0 + 512],
                             start=st, stop=False))
                p.op("tensor", lambda h, bk=bk, t0=t0: h.matmul(
                    P[bk][:, 0:512], m128[0:1, :], xsb[0:1, t0:t0 + 512],
                    start=False, stop=True), inc="pg")
            guard[("wb", sl)] = p.semv["pg"]
            p.wait("scalar", "pg")
            if "gs" in guard:
                p.wait("scalar", "ev", guard["gs"])
            p.op("scalar", lambda h, bg=bg, ic=ic: h.activation(
                out=gs[:, 0:512], in_=P[bg][:, 0:512], func=AF.Silu,
                scale=gcol[:, ic:ic + 1]), inc="sc")
            p.wait("vector", "sc")
            p.op("vector", lambda h, bu=bu, ic=ic: h.scalar_tensor_tensor(
                out=hb[:, ic * 512:(ic + 1) * 512], in0=P[bu][:, 0:512],
                scalar=ucol[:, ic:ic + 1], in1=gs[:, 0:512],
                op0=ALU.mult, op1=ALU.mult), inc="ev")
            guard[("evac", bg)] = p.semv["ev"]
            guard[("evac", bu)] = p.semv["ev"]
            guard["gs"] = p.semv["ev"]
        p.wait("tensor", "ev")
        if ("hs",) in guard:
            p.wait("tensor", "dv", guard[("hs",)])
        for c in range(NIC):
            p.op("tensor", lambda h, c=c, st=(c == 0), sp=(c == NIC - 1): h.matmul(
                pb[0:1, :], s.ones_l[:, 0:1], hb[:, c * 512:(c + 1) * 512],
                start=st, stop=sp), inc="pe" if c == NIC - 1 else None)
        p.wait("vector", "pe")
        p.op("vector", lambda h, t0=t0: h.tensor_copy(out=hsb[0:1, 0:512],
                                                      in_=pb[0:1, :]), inc="dv")
        guard[("hs",)] = p.semv["dv"]
        banks = [6, 7, 0, 1]
        for og in range(4):
            for b in banks:
                if ("evac", b) in guard:
                    p.wait("tensor", "ev", guard[("evac", b)])
            p.wait("tensor", "dv", guard[("hs",)])
            for ic in range(NIC):
                sl = ic % 2
                if ("wd8", sl) in guard:
                    p.wait("sync", "cv", guard[("wd8", sl)])
                p.dma("sync", wd8[sl][:],
                      dwT[ic * 128:(ic + 1) * 128, og * 512:(og + 1) * 512], f"wde{sl}")
                p.wait("gpsimd", f"wde{sl}")
                if ("wdb", sl) in guard:
                    p.wait("gpsimd", "pg", guard[("wdb", sl)])
                p.op("gpsimd", lambda h, sl=sl: h.tensor_copy(out=wdb[sl], in_=wd8[sl][:]),
                     inc="cv")
                guard[("wd8", sl)] = p.semv["cv"]
                p.wait("tensor", "cv")
                for j in range(4):
                    p.op("tensor", lambda h, b=banks[j], j=j, sl=sl, ic=ic: h.matmul(
                        P[b][:, 0:512], wdb[sl][:, j * 128:(j + 1) * 128],
                        hb[:, ic * 512:(ic + 1) * 512],
                        start=(ic == 0), stop=False),
                        inc="pg" if j == 3 else None)
                guard[("wdb", sl)] = p.semv["pg"]
            for j in range(4):
                p.op("tensor", lambda h, b=banks[j]: h.matmul(
                    P[b][:, 0:512], m128[0:1, :], hsb[0:1, 0:512],
                    start=False, stop=True), inc="pg" if j == 3 else None)
            chains_done = p.semv["pg"]
            for j in range(4):
                oc = og * 4 + j
                xs = slice((j % 2) * 512, (j % 2) * 512 + 512)
                if ("xq", j % 2) in guard:
                    p.wait("sync", "ev", guard[("xq", j % 2)])
                p.dma("sync", xq[:, xs], xT[oc * 128:(oc + 1) * 128, t0:t0 + 512],
                      f"ld{j % 2}")
                p.wait("vector", "pg", chains_done)
                p.wait("vector", f"ld{j % 2}")
                p.wait("vector", f"yo{j % 2}")
                p.op("vector", lambda h, b=banks[j], xs=xs, oc=oc: h.scalar_tensor_tensor(
                    out=oute[:, xs], in0=P[b][:, 0:512], scalar=dcolh[:, oc:oc + 1],
                    in1=xq[:, xs], op0=ALU.mult, op1=ALU.add), inc="ev")
                guard[("evac", banks[j])] = p.semv["ev"]
                guard[("xq", j % 2)] = p.semv["ev"]
                p.wait("gpsimd", "ev")
                p.dma("gpsimd", yT[oc * 128:(oc + 1) * 128, t0:t0 + 512],
                      oute[:, xs], f"yo{j % 2}")
    p.wait("gpsimd", "yo0")
    p.wait("gpsimd", "yo1")
    return p.build()


def stage_l1ffn_inputs(x_slice_T, gate_w, gate_scale, up_w, up_scale, down_w, down_scale):
    return {
        "xT": np.ascontiguousarray(x_slice_T, dtype=np.float32),
        "gwT": np.ascontiguousarray(np.asarray(gate_w).T).astype(np.uint8),
        "uwT": np.ascontiguousarray(np.asarray(up_w).T).astype(np.uint8),
        "dwT": np.ascontiguousarray(np.asarray(down_w).T).astype(np.uint8),
        "gsc": np.ascontiguousarray(np.asarray(gate_scale), dtype=np.float32).reshape(-1, 1),
        "usc": np.ascontiguousarray(np.asarray(up_scale), dtype=np.float32).reshape(-1, 1),
        "dsc": np.ascontiguousarray(np.asarray(down_scale), dtype=np.float32).reshape(-1, 1),
    }



def build_attn_front(TT=1024):
    D, NCH = 2048, 16
    p = Prog("afront")
    nc = p.nc
    xT = nc.dram_tensor("xT", [D, TT], F32, kind="ExternalInput").ap()
    wqkvT = nc.dram_tensor("wqkvT", [D, 3 * D], F32, kind="ExternalInput").ap()
    qnw = nc.dram_tensor("qnw", [D, 1], F32, kind="ExternalInput").ap()
    knw = nc.dram_tensor("knw", [D, 1], F32, kind="ExternalInput").ap()
    cosb = nc.dram_tensor("cosb", [128, TT], F32, kind="ExternalInput").ap()
    sinb = nc.dram_tensor("sinb", [128, TT], F32, kind="ExternalInput").ap()
    permT = nc.dram_tensor("permT", [128, 128], BF16, kind="ExternalInput").ap()
    qT = nc.dram_tensor("qT", [D, TT], BF16, kind="ExternalOutput").ap()
    kT = nc.dram_tensor("kT", [D, TT], BF16, kind="ExternalOutput").ap()
    vT = nc.dram_tensor("vT", [D, TT], BF16, kind="ExternalOutput").ap()

    for sn in ("ld0", "ld1", "ldc", "wd0", "wd1", "cv", "dv", "pe", "sc", "ev",
               "pg", "yo0", "yo1", "yo2", "yo3"):
        p.sem(sn)

    s = NS()
    AB = 58 * 1024
    s.A = p.sbuf("A", [128, AB], BF16)
    s.xb = s.A[:, 0:NCH * TT]
    s.xn = s.A[:, NCH * TT:2 * NCH * TT]
    q2 = s.A[:, 2 * NCH * TT: 2 * NCH * TT + TT]
    qnb = s.A[:, 2 * NCH * TT + TT: 2 * NCH * TT + 2 * TT]
    o_q = 2 * NCH * TT + 2 * TT
    qf = s.A[:, o_q: o_q + NCH * TT]
    o_w = o_q + NCH * TT
    wslab_b = [s.A[:, o_w + k * 2048: o_w + (k + 1) * 2048] for k in range(2)]
    outb = [s.A[:, o_w + 2 * 2048 + k * 512: o_w + 2 * 2048 + (k + 1) * 512]
            for k in range(4)]
    permb = s.A[:, o_w + 2 * 2048 + 4 * 512: o_w + 2 * 2048 + 4 * 512 + 128]
    assert o_w + 2 * 2048 + 4 * 512 + 128 <= AB
    wslab_f = [p.sbuf(f"wsf{k}", [128, 2048], F32) for k in range(2)]
    s.xtmp = p.sbuf("xtmp", [128, 2 * TT], F32)
    s.ones_l = p.sbuf("ones_l", [128, 1], BF16)
    s.ones_c = p.sbuf("ones_c", [1, 128], F32)
    s.epsc = p.sbuf("epsc", [128, 1], F32)
    s.eps6 = p.sbuf("eps6", [128, 1], F32)
    s.r0 = p.sbuf("r0", [1, TT], F32)
    s.r1 = p.sbuf("r1", [1, TT], F32)
    s.r2 = p.sbuf("r2", [1, TT], F32)
    s.r3 = p.sbuf("r3", [1, TT], F32)
    s.mb = p.sbuf("mb", [128, TT], F32)
    s.rb = p.sbuf("rb", [128, TT], F32)
    cosf = p.sbuf("cosf", [128, TT], F32)
    sinf = p.sbuf("sinf", [128, TT], F32)
    nwcol = p.sbuf("nwcol", [128, 2 * NCH], F32)
    t1 = p.sbuf("t1", [128, TT], F32)
    t2 = p.sbuf("t2", [128, TT], F32)
    qsw = p.sbuf("qsw", [128, TT], F32)
    P = [p.psum(f"P{i}", [128, 512]) for i in range(8)]
    pa, pb = P[0], P[1]

    p.op("vector", lambda h: h.memset(s.ones_l[:], 1.0))
    p.op("vector", lambda h: h.memset(s.epsc[:], 1e-5))
    p.op("vector", lambda h: h.memset(s.eps6[:], 1e-6))
    p.op("vector", lambda h: h.memset(s.ones_c[:], 1.0), inc="dv")
    p.dma("sync", cosf[:], cosb, "ldc")
    p.dma("sync", sinf[:], sinb, "ldc")
    p.dma("sync", permb, permT, "ldc")
    p.dma("sync", nwcol[:, 0:NCH], qnw.rearrange("(k p) f -> p (k f)", p=128),
          "ldc", slow=True)
    p.dma("sync", nwcol[:, NCH:2 * NCH], knw.rearrange("(k p) f -> p (k f)", p=128),
          "ldc", slow=True)
    p.wait("vector", "ldc")
    p.wait("tensor", "ldc")
    emit_load_x_as_bf16(p, s, xT, TT, NCH)
    emit_ln(p, s, pa, pb, TT, NCH)

    guard = {}
    def qkv_chunk(oc, bk_pair, out_cb):
        sl = oc % 2
        if ("w", sl) in guard:
            p.wait("sync", "cv", guard[("w", sl)])
        p.dma("sync", wslab_f[sl].rearrange("p (k i) -> p k i", k=16),
              wqkvT[:, oc * 128:(oc + 1) * 128].rearrange("(k p) i -> p k i", p=128),
              f"wd{sl}")
        p.wait("vector", f"wd{sl}")
        if ("wb", sl) in guard:
            p.wait("vector", "pg", guard[("wb", sl)])
        p.op("vector", lambda h, sl=sl: h.tensor_copy(out=wslab_b[sl], in_=wslab_f[sl][:]),
             inc="cv")
        guard[("w", sl)] = p.semv["cv"]
        p.wait("tensor", "cv")
        p.wait("tensor", "dv")
        for th in range(2):
            bk = bk_pair[th]
            if ("evac", bk) in guard:
                p.wait("tensor", "ev", guard[("evac", bk)])
            for k in range(NCH):
                p.op("tensor", lambda h, k=k, sl=sl, bk=bk, th=th,
                     st=(k == 0), sp=(k == NCH - 1): h.matmul(
                         P[bk][:, 0:512], wslab_b[sl][:, k * 128:(k + 1) * 128],
                         s.xn[:, k * TT + th * 512: k * TT + th * 512 + 512],
                         start=st, stop=sp), inc="pg" if k == NCH - 1 else None)
        guard[("wb", sl)] = p.semv["pg"]
        for th in range(2):
            out_cb(bk_pair[th], th)

    for oc in range(32, 48):
        def v_out(bk, th, oc=oc):
            slot = 2 * (oc % 2) + th
            p.wait("scalar", "pg")
            p.wait("scalar", f"yo{slot}")
            p.op("scalar", lambda h, bk=bk, slot=slot: h.copy(
                out=outb[slot], in_=P[bk][:, 0:512]), inc="ev")
            guard[("evac", bk)] = p.semv["ev"]
            p.wait("gpsimd", "ev")
            p.dma("gpsimd", vT[(oc - 32) * 128:(oc - 31) * 128, th * 512:th * 512 + 512],
                  outb[slot], f"yo{slot}")
        qkv_chunk(oc, (2 + 2 * (oc % 2), 3 + 2 * (oc % 2) - 1 + 1), v_out)

    p.wait("vector", "yo2")
    p.wait("vector", "yo3")
    for which, base_oc, wofs, outT in (("q", 0, 0, qT), ("k", 16, NCH, kT)):
        for oc in range(base_oc, base_oc + 16):
            def f_out(bk, th, oc=oc):
                c = oc - base_oc
                p.wait("vector", "pg")
                p.op("vector", lambda h, bk=bk, th=th, c=c: h.tensor_copy(
                    out=qf[:, c * TT + th * 512: c * TT + th * 512 + 512],
                    in_=P[bk][:, 0:512]), inc="ev")
                guard[("evac", bk)] = p.semv["ev"]
            qkv_chunk(oc, (2 + 2 * (oc % 2), 3 + 2 * (oc % 2) - 1 + 1), f_out)
        for th in range(2):
            p.wait("vector", "ev")
            for c in range(NCH):
                sl = slice(c * TT + th * 512, c * TT + th * 512 + 512)
                q2s = slice((c % 2) * 512, (c % 2) * 512 + 512)
                if c >= 2:
                    p.wait("vector", "pe", guard[("q2", c % 2)])
                p.op("vector", lambda h, sl=sl, q2s=q2s: h.tensor_tensor(
                    out=q2[:, q2s], in0=qf[:, sl], in1=qf[:, sl], op=ALU.mult),
                    inc="dv")
                p.wait("tensor", "dv")
                p.op("tensor", lambda h, q2s=q2s, st=(c == 0), sp=(c == NCH - 1): h.matmul(
                    pa[0:1, :], s.ones_l[:, 0:1], q2[:, q2s],
                    start=st, stop=sp), inc="pe")
                guard[("q2", c % 2)] = p.semv["pe"]
            p.wait("vector", "pe")
            p.op("vector", lambda h, th=th: h.tensor_scalar(
                out=s.r1[0:1, th * 512:(th + 1) * 512], in0=pa[0:1, :],
                scalar1=1.0 / D, scalar2=None, op0=ALU.mult), inc="dv")
        p.wait("scalar", "dv")
        p.op("scalar", lambda h: h.activation(out=s.r2[0:1, 0:TT], in_=s.r1[0:1, 0:TT],
                                              func=AF.Ln, bias=s.eps6[0:1, 0:1]))
        p.op("scalar", lambda h: h.activation(out=s.r3[0:1, 0:TT], in_=s.r2[0:1, 0:TT],
                                              func=AF.Exp, scale=-0.5), inc="sc")
        for th in range(2):
            p.wait("tensor", "sc")
            p.wait("tensor", "dv")
            p.op("tensor", lambda h, th=th: h.matmul(
                pb[:, 0:512], s.ones_c[0:1, :], s.r3[0:1, th * 512:(th + 1) * 512],
                start=True, stop=True), inc="pe")
            p.wait("vector", "pe")
            p.op("vector", lambda h, th=th: h.tensor_copy(
                out=s.rb[:, th * 512:(th + 1) * 512], in_=pb[:, 0:512]), inc="dv")
        for c in range(NCH):
            csl = slice(c * TT, (c + 1) * TT)
            p.op("vector", lambda h, csl=csl, c=c, wofs=wofs: h.scalar_tensor_tensor(
                out=qf[:, csl], in0=qf[:, csl], scalar=nwcol[:, wofs + c: wofs + c + 1],
                in1=s.rb[:, 0:TT], op0=ALU.mult, op1=ALU.mult))
            p.op("vector", lambda h, csl=csl: h.tensor_copy(out=qnb, in_=qf[:, csl]),
                 inc="dv")
            for th in range(2):
                p.wait("tensor", "dv")
                if ("psw", th) in guard:
                    p.wait("tensor", "ev", guard[("psw", th)])
                p.op("tensor", lambda h, th=th: h.matmul(
                    P[6 + th][:, 0:512], permb, qnb[th * 512:(th + 1) * 512]
                    if False else qnb[:, th * 512:(th + 1) * 512],
                    start=True, stop=True), inc="pe")
                p.wait("vector", "pe")
                p.op("vector", lambda h, th=th: h.tensor_copy(
                    out=qsw[:, th * 512:(th + 1) * 512], in_=P[6 + th][:, 0:512]),
                    inc="ev")
                guard[("psw", th)] = p.semv["ev"]
            p.op("vector", lambda h, csl=csl: h.tensor_tensor(
                out=t1[:], in0=qf[:, csl], in1=cosf[:], op=ALU.mult))
            p.op("vector", lambda h: h.tensor_tensor(
                out=t2[:], in0=qsw[:], in1=sinf[:], op=ALU.mult))
            slot = c % 2
            p.wait("vector", f"yo{slot}")
            p.op("vector", lambda h, slot=slot: h.tensor_tensor(
                out=outb[slot], in_=None, in0=t1[:, 0:512], in1=t2[:, 0:512],
                op=ALU.add) if False else h.tensor_tensor(
                out=outb[slot], in0=t1[:, 0:512], in1=t2[:, 0:512], op=ALU.add))
            p.op("vector", lambda h, slot=slot: h.tensor_tensor(
                out=outb[2 + slot], in0=t1[:, 512:1024], in1=t2[:, 512:1024],
                op=ALU.add), inc="dv")
            p.wait("gpsimd", "dv")
            p.dma("gpsimd", outT[c * 128:(c + 1) * 128, 0:512], outb[slot], f"yo{slot}")
            p.dma("gpsimd", outT[c * 128:(c + 1) * 128, 512:1024], outb[2 + slot],
                  f"yo{slot}")
    for sn in ("yo0", "yo1", "yo2", "yo3"):
        p.wait("gpsimd", sn)
    return p.build()



def build_attn_core(NH=2, T=8192, W=1024):
    KB = T + W
    NQ = T // 128
    p = Prog("acore")
    nc = p.nc
    qh = nc.dram_tensor("qh", [NH * 128, T], BF16, kind="ExternalInput").ap()
    kh = nc.dram_tensor("kh", [NH * 128, T], BF16, kind="ExternalInput").ap()
    vh = nc.dram_tensor("vh", [NH * 128, T], BF16, kind="ExternalInput").ap()
    maskb = nc.dram_tensor("maskb", [128, 9 * 1152], F32, kind="ExternalInput").ap()
    ident = nc.dram_tensor("ident", [128, 128], BF16, kind="ExternalInput").ap()
    oh = nc.dram_tensor("oh", [NH * 128, T], BF16, kind="ExternalOutput").ap()

    for sn in ("ld", "dv", "pe", "sc", "ev", "pg", "yo0", "yo1"):
        p.sem(sn)

    qb_t = p.sbuf("qb_t", [128, T], BF16)
    kbuf = p.sbuf("kbuf", [128, KB], BF16)
    vbuf = p.sbuf("vbuf", [128, KB], BF16)
    VT = p.sbuf("VT", [128, KB], BF16)
    maskt = p.sbuf("maskt", [128, 9 * 1152], F32)
    identt = p.sbuf("identt", [128, 128], BF16)
    S = p.sbuf("S", [128, 1152], F32)
    Pb = p.sbuf("Pb", [128, 1152], BF16)
    PT = p.sbuf("PT", [128, 9 * 128], BF16)
    mx = p.sbuf("mx", [128, 4], F32)
    ob = p.sbuf("ob", [128, 2 * 128], BF16)
    scol = p.sbuf("scol", [128, 1], F32)
    P = [p.psum(f"P{i}", [128, 512]) for i in range(5)]
    ptr = p.psum("ptr", [128, 128], BF16)
    pvt = p.psum("pvt", [128, 128], BF16)

    p.dma("sync", maskt[:], maskb, "ld")
    p.dma("sync", identt[:], ident, "ld")
    p.op("vector", lambda h: h.memset(scol[:], 2 ** -3.5), inc="dv")

    guard = {}
    for head in range(NH):
        if head > 0:
            p.wait("sync", "pg")
            p.wait("sync", "ev")
        r0 = head * 128
        p.dma("sync", qb_t[:], qh[r0:r0 + 128, :], "ld")
        p.dma("sync", kbuf[:, W:KB], kh[r0:r0 + 128, :], "ld")
        p.dma("sync", vbuf[:, W:KB], vh[r0:r0 + 128, :], "ld")
        p.wait("vector", "ld")
        if head == 0:
            p.op("vector", lambda h: h.memset(kbuf[:, 0:W], 0.0))
            p.op("vector", lambda h: h.memset(vbuf[:, 0:W], 0.0), inc="dv")
        for kc in range(KB // 128):
            if ("vt",) in guard:
                p.wait("tensor", "ev", guard[("vt",)])
            p.wait("tensor", "ld")
            p.wait("tensor", "dv")
            p.op("tensor", lambda h, kc=kc: h.transpose(
                pvt[:], vbuf[:, kc * 128:(kc + 1) * 128], identt[:]), inc="pe")
            p.wait("vector", "pe")
            p.op("vector", lambda h, kc=kc: h.tensor_copy(
                out=VT[:, kc * 128:(kc + 1) * 128], in_=pvt[:]), inc="ev")
            guard[("vt",)] = p.semv["ev"]
        for qb in range(NQ):
            var = min(qb, 8)
            k0 = qb * 128
            segs = [(0, 512, 0), (512, 512, 1), (1024, 128, 2)]
            for (o, n, bk) in segs:
                if ("sb", bk) in guard:
                    p.wait("tensor", "ev", guard[("sb", bk)])
                p.op("tensor", lambda h, o=o, n=n, bk=bk, k0=k0, qb=qb: h.matmul(
                    P[bk][:, 0:n], qb_t[:, qb * 128:(qb + 1) * 128],
                    kbuf[:, k0 + o: k0 + o + n], start=True, stop=True),
                    inc="pg" if bk == 2 else None)
            p.wait("vector", "pg")
            for (o, n, bk) in segs:
                p.op("vector", lambda h, o=o, n=n, bk=bk, var=var: h.scalar_tensor_tensor(
                    out=S[:, o:o + n], in0=P[bk][:, 0:n], scalar=scol[:, 0:1],
                    in1=maskt[:, var * 1152 + o: var * 1152 + o + n],
                    op0=ALU.mult, op1=ALU.add), inc="ev" if bk == 2 else None)
            guard[("sb", 0)] = guard[("sb", 1)] = guard[("sb", 2)] = p.semv["ev"]
            p.op("vector", lambda h: h.reduce_max(out=mx[:, 0:1], in_=S[:], axis=AX.X))
            p.op("vector", lambda h: h.tensor_scalar(
                out=mx[:, 1:2], in0=mx[:, 0:1], scalar1=-1.0, scalar2=None,
                op0=ALU.mult), inc="dv")
            p.wait("scalar", "dv")
            p.op("scalar", lambda h: h.activation(
                out=Pb[:], in_=S[:], func=AF.Exp, bias=mx[:, 1:2]), inc="sc")
            p.wait("vector", "sc")
            p.op("vector", lambda h: h.reduce_sum(out=mx[:, 2:3], in_=Pb[:], axis=AX.X))
            p.op("vector", lambda h: h.reciprocal(out=mx[:, 3:4], in_=mx[:, 2:3]))
            p.op("vector", lambda h: h.tensor_scalar(
                out=Pb[:], in0=Pb[:], scalar1=mx[:, 3:4], scalar2=None,
                op0=ALU.mult), inc="dv")
            for kc in range(9):
                p.wait("tensor", "dv")
                if ("pt",) in guard:
                    p.wait("tensor", "ev", guard[("pt",)])
                p.op("tensor", lambda h, kc=kc: h.transpose(
                    ptr[:], Pb[:, kc * 128:(kc + 1) * 128], identt[:]), inc="pe")
                p.wait("vector", "pe")
                p.op("vector", lambda h, kc=kc: h.tensor_copy(
                    out=PT[:, kc * 128:(kc + 1) * 128], in_=ptr[:]), inc="ev")
                guard[("pt",)] = p.semv["ev"]
            if ("ob",) in guard:
                p.wait("tensor", "ev", guard[("ob",)])
            p.wait("tensor", "ev", guard[("pt",)])
            for kc in range(9):
                p.op("tensor", lambda h, kc=kc, qb=qb: h.matmul(
                    P[4][:, 0:128], VT[:, (qb + kc) * 128:(qb + kc + 1) * 128],
                    PT[:, kc * 128:(kc + 1) * 128],
                    start=(kc == 0), stop=(kc == 8)), inc="pg" if kc == 8 else None)
            sl = qb % 2
            p.wait("vector", "pg")
            p.wait("vector", f"yo{sl}")
            p.op("vector", lambda h, sl=sl: h.tensor_copy(
                out=ob[:, sl * 128:(sl + 1) * 128], in_=P[4][:, 0:128]), inc="ev")
            guard[("ob",)] = p.semv["ev"]
            p.wait("gpsimd", "ev")
            p.dma("gpsimd", oh[r0:r0 + 128, qb * 128:(qb + 1) * 128],
                  ob[:, sl * 128:(sl + 1) * 128], f"yo{sl}")
    p.wait("gpsimd", "yo0")
    p.wait("gpsimd", "yo1")
    return p.build()


def make_attn_masks():
    m = np.full((9, 128, 1152), -1e30, dtype=np.float32)
    for v in range(9):
        for ql in range(128):
            lo = ql + 1
            if v < 8:
                lo = max(lo, 1024 - v * 128)
            m[v, ql, lo:ql + 1025] = 0.0
    return np.ascontiguousarray(m.transpose(1, 0, 2).reshape(128, 9 * 1152))


def make_rope_tables(t0, TT):
    pos = np.arange(t0, t0 + TT, dtype=np.float64)
    inv = 1.0 / (10000.0 ** (np.arange(0, 128, 2, dtype=np.float64) / 128.0))
    fr = pos[None, :] * inv[:, None]
    cos = np.cos(fr).astype(np.float32)
    sin = np.sin(fr).astype(np.float32)
    cosb = np.concatenate([cos, cos], axis=0)
    sinb = np.concatenate([sin, sin], axis=0)
    return np.ascontiguousarray(cosb), np.ascontiguousarray(sinb)


def make_perm():
    Pm = np.zeros((128, 128), dtype=np.float32)
    for d in range(64):
        Pm[d, 64 + d] = -1.0
        Pm[64 + d, d] = 1.0
    return np_bf16(Pm.T)



from concourse.bass_utils import run_bass_kernel_spmd

_PROG_CACHE = {}


def _prog(name, fn):
    if name not in _PROG_CACHE:
        _PROG_CACHE[name] = fn()
    return _PROG_CACHE[name]


def _launch(nc, in_maps):
    return run_bass_kernel_spmd(nc, in_maps, core_ids=list(range(8))).results


def _tslices(aT):
    return [np.ascontiguousarray(aT[:, c * 1024:(c + 1) * 1024]) for c in range(8)]


def _cat_t(rs, key):
    return np.concatenate([r[key] for r in rs], axis=1)


def _cat_c(rs, key):
    return np.concatenate([r[key] for r in rs], axis=0)


def kernel(L0_attn_x, L0_attn_in_proj_w, L0_attn_conv_w, L0_attn_conv_b,
           L0_attn_x_proj_w, L0_attn_dt_proj_w, L0_attn_dt_proj_b, L0_attn_A_log,
           L0_attn_D, L0_attn_out_proj_w, L0_ffn_lut, L0_ffn_gate_idx, L0_ffn_up_idx,
           L0_ffn_down_idx, L1_attn_w_qkv, L1_attn_q_norm_w, L1_attn_k_norm_w,
           L1_attn_w_o, L1_ffn_gate_w, L1_ffn_gate_scale, L1_ffn_up_w,
           L1_ffn_up_scale, L1_ffn_down_w, L1_ffn_down_scale):
    x0T = np.ascontiguousarray(np.asarray(L0_attn_x)[0].T.astype(np.float32))

    ghost_w = {
        "gidxT": np.ascontiguousarray(np.asarray(L0_ffn_gate_idx).T).astype(np.uint8),
        "uidxT": np.ascontiguousarray(np.asarray(L0_ffn_up_idx).T).astype(np.uint8),
        "didxT": np.ascontiguousarray(np.asarray(L0_ffn_down_idx).T).astype(np.uint8),
        "lut": np.ascontiguousarray(np.asarray(L0_ffn_lut), dtype=np.float32).reshape(1, 16),
    }
    wipT = np.ascontiguousarray(np.asarray(L0_attn_in_proj_w).T.astype(np.float32))
    convw = np.ascontiguousarray(np.asarray(L0_attn_conv_w)[:, 0, :].astype(np.float32))
    convb = np.ascontiguousarray(np.asarray(L0_attn_conv_b).astype(np.float32).reshape(-1, 1))
    xpT = np.ascontiguousarray(np.asarray(L0_attn_x_proj_w).T.astype(np.float32))
    dtpT = np.ascontiguousarray(np.asarray(L0_attn_dt_proj_w).T.astype(np.float32))
    dtb = np.ascontiguousarray(np.asarray(L0_attn_dt_proj_b).astype(np.float32).reshape(-1, 1))
    Alog = np.ascontiguousarray(np.asarray(L0_attn_A_log).astype(np.float32))
    Dv = np.ascontiguousarray(np.asarray(L0_attn_D).astype(np.float32).reshape(-1, 1))
    wopT = np.ascontiguousarray(np.asarray(L0_attn_out_proj_w).T.astype(np.float32))
    l1w = dict(
        gwT=np.ascontiguousarray(np.asarray(L1_ffn_gate_w).T).astype(np.uint8),
        uwT=np.ascontiguousarray(np.asarray(L1_ffn_up_w).T).astype(np.uint8),
        dwT=np.ascontiguousarray(np.asarray(L1_ffn_down_w).T).astype(np.uint8),
        gsc=np.ascontiguousarray(np.asarray(L1_ffn_gate_scale), dtype=np.float32).reshape(-1, 1),
        usc=np.ascontiguousarray(np.asarray(L1_ffn_up_scale), dtype=np.float32).reshape(-1, 1),
        dsc=np.ascontiguousarray(np.asarray(L1_ffn_down_scale), dtype=np.float32).reshape(-1, 1),
    )
    wqkvT = np.ascontiguousarray(np.asarray(L1_attn_w_qkv).T.astype(np.float32))
    qnw = np.ascontiguousarray(np.asarray(L1_attn_q_norm_w).astype(np.float32).reshape(-1, 1))
    knw = np.ascontiguousarray(np.asarray(L1_attn_k_norm_w).astype(np.float32).reshape(-1, 1))
    woT = np.ascontiguousarray(np.asarray(L1_attn_w_o).T.astype(np.float32))
    selc = make_sel()
    masks = make_attn_masks()
    identc = np_bf16(np.eye(128))
    permc = make_perm()

    def ghost(xT):
        nc = _prog("ghost", lambda: build_ghost(1024))
        rs = _launch(nc, [dict(xT=sl, **ghost_w) for sl in _tslices(xT)])
        return _cat_t(rs, "yT")

    def l1ffn(xT):
        nc = _prog("l1ffn", lambda: build_l1ffn(1024))
        rs = _launch(nc, [dict(xT=sl, **l1w) for sl in _tslices(xT)])
        return _cat_t(rs, "yT")

    x1T = ghost(x0T)

    nc = _prog("mfront", lambda: build_mamba_front(1024))
    rs = _launch(nc, [dict(xT=sl, wipT=wipT) for sl in _tslices(x1T)])
    hinT = _cat_t(rs, "hinT")
    sgT = _cat_t(rs, "sgT")

    nc = _prog("conv", lambda: build_conv(512, 8192))
    rs = _launch(nc, [dict(hinT=np.ascontiguousarray(hinT[c * 512:(c + 1) * 512]),
                           convw=np.ascontiguousarray(convw[c * 512:(c + 1) * 512]),
                           convb=np.ascontiguousarray(convb[c * 512:(c + 1) * 512]))
                      for c in range(8)])
    uT = _cat_c(rs, "uT")

    nc = _prog("xproj", lambda: build_xproj_dt(1024))
    rs = _launch(nc, [dict(uT=np.ascontiguousarray(uT[:, c * 1024:(c + 1) * 1024]),
                           xpT=xpT, dtpT=dtpT, dtb=dtb) for c in range(8)])
    dtT = _cat_t(rs, "dtT")
    Bf = _cat_t(rs, "Bo")
    Cf = _cat_t(rs, "Co")

    nc = _prog("scan", lambda: build_scan(512, 8192, 2048))
    rs = _launch(nc, [dict(uT=np.ascontiguousarray(uT[c * 512:(c + 1) * 512]),
                           sgT=np.ascontiguousarray(sgT[c * 512:(c + 1) * 512]),
                           dtT=np.ascontiguousarray(dtT[c * 512:(c + 1) * 512]),
                           Bf=Bf, Cf=Cf,
                           Alog=np.ascontiguousarray(Alog[c * 512:(c + 1) * 512]),
                           Dv=np.ascontiguousarray(Dv[c * 512:(c + 1) * 512]),
                           sel=selc) for c in range(8)])
    ygT = _cat_c(rs, "ygT")

    nc = _prog("outp4", lambda: build_outproj(1024, 4096))
    rs = _launch(nc, [dict(ygT=np.ascontiguousarray(ygT[:, c * 1024:(c + 1) * 1024]),
                           wopT=wopT, xT=sl)
                      for c, sl in enumerate(_tslices(x1T))])
    x2T = _cat_t(rs, "yT")

    x3T = ghost(x2T)

    x4T = l1ffn(x3T)

    nc = _prog("afront", lambda: build_attn_front(1024))
    in_maps = []
    for c, sl in enumerate(_tslices(x4T)):
        cosb, sinb = make_rope_tables(c * 1024, 1024)
        in_maps.append(dict(xT=sl, wqkvT=wqkvT, qnw=qnw, knw=knw,
                            cosb=cosb, sinb=sinb, permT=permc))
    rs = _launch(nc, in_maps)
    qT = _cat_t(rs, "qT")
    kT = _cat_t(rs, "kT")
    vT = _cat_t(rs, "vT")

    nc = _prog("acore", lambda: build_attn_core(2, 8192, 1024))
    rs = _launch(nc, [dict(qh=np.ascontiguousarray(qT[c * 256:(c + 1) * 256]),
                           kh=np.ascontiguousarray(kT[c * 256:(c + 1) * 256]),
                           vh=np.ascontiguousarray(vT[c * 256:(c + 1) * 256]),
                           maskb=masks, ident=identc) for c in range(8)])
    attnT = _cat_c(rs, "oh")

    nc = _prog("outp2", lambda: build_outproj(1024, 2048))
    rs = _launch(nc, [dict(ygT=np.ascontiguousarray(attnT[:, c * 1024:(c + 1) * 1024]),
                           wopT=woT, xT=sl)
                      for c, sl in enumerate(_tslices(x4T))])
    x5T = _cat_t(rs, "yT")

    x6T = l1ffn(x5T)
    return np.ascontiguousarray(x6T.T).reshape(1, 8192, 2048).astype(np.float32)
